# version 01
# baseline (speedup 1.0000x reference)
import math

import numpy as np
import ml_dtypes

import concourse.bass as bass
import concourse.mybir as mybir
from concourse.bass_utils import run_bass_kernel_spmd

F32 = mybir.dt.float32
BF16 = mybir.dt.bfloat16
U8 = mybir.dt.uint8
ALU = mybir.AluOpType
AX = mybir.AxisListType
AF = mybir.ActivationFunctionType

B = 16
S = 2048
H = 1024
NH = 16
HD = 64
II = 2816
NCORES = 8
SEQ_PER_CORE = B // NCORES
TT = S // 128
FC = H // 128
ICC = II // 128
STRIDE = max(1, int(math.isqrt(S)))
EXPB = 12.0
SCALE = HD ** -0.5

RC0 = 1.5 * (2.0 ** 23) * 0.5
RC1 = 1.5 * (2.0 ** 23) * 1.0
RC2 = 1.5 * (2.0 ** 23) * 2.0

QCW = 1024
NQC = S // QCW


class Sem:
    __slots__ = ("name", "count", "h")

    def __init__(self, name):
        self.name = name
        self.count = 0
        self.h = None


def ts_(e, out, in0, s1, s2, op0, op1=None):
    if s2 is None:
        return e.tensor_single_scalar(out, in0, s1, op0)
    return e.tensor_scalar(out, in0, s1, s2, op0, op1)


class Builder:
    def __init__(self, debug=False):
        self.nc = bass.Bass("TRN2", target_bir_lowering=False)
        self.debug = debug
        self.ops = {"sync": [], "scalar": [], "vector": [], "tensor": [], "gpsimd": []}
        self.sems = {}
        self._vsb = []
        self._off = 0
        self._ovl_base = None
        self._ovl_max = 0
        self.guard = {}

    def sem(self, name):
        if name not in self.sems:
            self.sems[name] = Sem(name)
        return self.sems[name]

    def _alloc(self, name, shape, dtype, align=64):
        per = int(np.prod(shape[1:])) * mybir.dt.size(dtype)
        off = (self._off + align - 1) // align * align
        self._off = off + per
        h = self.nc.virtual_sbuf_tensor(name, list(shape), dtype)
        self._vsb.append((h, off))
        return h

    def alias_sbuf(self, name, shape, dtype, of_handle):
        off = next(o for h, o in self._vsb if h is of_handle)
        h = self.nc.virtual_sbuf_tensor(name, list(shape), dtype)
        self._vsb.append((h, off))
        return h

    def overlay_begin(self):
        self._ovl_base = self._off

    def overlay_reset(self):
        self._ovl_max = max(self._ovl_max, self._off)
        self._off = self._ovl_base

    def finalize_arena(self):
        self._ovl_max = max(self._ovl_max, self._off)
        size = (self._ovl_max + 63) // 64 * 64
        assert size <= 206 * 1024, f"arena too big: {size} B/partition"
        nc = self.nc
        self._backing_cm = nc.sbuf_tensor("arena", [128, size], U8)
        self._backing = self._backing_cm.__enter__()
        base = nc.lookup_mloc(self._backing).addr
        for h, off in self._vsb:
            nc._tensor(h.name, h.shape, h.dtype, type="SB")
            mloc = nc.lookup_mloc(h)
            mloc.allocated = True
            mloc.addr = base + off
        return size

    def E(self, eng, fn, wait=(), inc=None):
        if inc is not None:
            inc[0].count += inc[1]
        self.ops[eng].append((fn, [w for w in wait if w is not None], inc))

    def dma(self, eng, out_ap, in_ap, semm, wait=()):
        self.E(eng, lambda e, o=out_ap, i=in_ap: e.dma_start(o, i), wait=wait,
               inc=(semm, 16))
        return semm.count

    def ps_wait(self, name):
        w = self.guard.get(name, [])
        self.guard[name] = []
        return w

    def ps_read(self, name, semm, count):
        self.guard.setdefault(name, []).append((semm, count))


def build_program(debug=False):
    b = Builder(debug)
    nc = b.nc
    E, sem = b.E, b.sem

    x_in = nc.dram_tensor("x", [SEQ_PER_CORE, S, H], F32, kind="ExternalInput").ap()
    w_in = {}
    for nm, shp in (("wq", [H, H]), ("wk", [H, H]), ("wv", [H, H]), ("wo", [H, H]),
                    ("wg", [II, H]), ("wu", [II, H]), ("wd", [H, II])):
        w_in[nm] = nc.dram_tensor(nm, shp, F32, kind="ExternalInput").ap()
    maskS = nc.dram_tensor("maskS", [S, S], BF16, kind="ExternalInput").ap()
    maskF = nc.dram_tensor("maskF", [S, S], BF16, kind="ExternalInput").ap()
    ident_in = nc.dram_tensor("ident", [128, 128], BF16, kind="ExternalInput").ap()
    out_hbm = nc.dram_tensor("out", [SEQ_PER_CORE, S, H], F32, kind="ExternalOutput").ap()
    x2s = nc.dram_tensor("x2s", [SEQ_PER_CORE, S, H], F32,
                         kind="ExternalOutput" if debug else "Internal").ap()
    wT = {}
    for nm, shp in (("wqT", [H, H]), ("wkT", [H, H]), ("wvT", [H, H]), ("woT", [H, H]),
                    ("wgT", [H, II]), ("wuT", [H, II]), ("wdT", [II, H])):
        wT[nm] = nc.dram_tensor(nm + "s", shp, BF16, kind="Internal").ap()
    dbg = {}
    if debug:
        for nm, shp, dt_ in (("d_bigT", [128, FC * S], BF16),
                             ("d_qtp", [128, S], BF16),
                             ("d_ktp", [128, S], BF16),
                             ("d_vaug", [128, TT * NH * 65], BF16),
                             ("d_atok", [128, TT * H], BF16),
                             ("d_ptm", [128, QCW], BF16),
                             ("d_ptm7", [128, QCW], BF16),
                             ("d_pv", [128, 520], F32),
                             ("d_wqT", [H, H], BF16)):
            dbg[nm] = nc.dram_tensor(nm, shp, dt_, kind="ExternalOutput").ap()

    A = b._alloc
    ident_sb = A("ident_sb", [128, 128], BF16)
    consts = A("consts", [128, 8], F32)
    lnsm = A("lnsm", [128, 16], F32)
    ln2m = A("ln2m", [128, SEQ_PER_CORE * TT], F32)
    ln2r = A("ln2r", [128, SEQ_PER_CORE * TT], F32)
    qs_bm = A("qs_bm", [128, II // 16], F32)
    qs_bmp = A("qs_bmp", [128, II // 16], F32)
    qs_inv = A("qs_inv", [128, II // 16], F32)
    qs_i6 = A("qs_i6", [128, II // 16], F32)
    qs_s2 = A("qs_s2", [128, II // 16], F32)
    xld = [A(f"xld{i}", [128, H], F32) for i in range(2)]
    x2w = [A(f"x2w{i}", [128, H], F32) for i in range(2)]
    pvr = A("pvr", [128, 16], F32)
    zbf = A("zbf", [128, 512], BF16)

    b.overlay_begin()
    q_aa = A("q_aa", [128, II], F32)
    q_vv = A("q_vv", [128, II], F32)
    q_w0 = A("q_w0", [128, II], F32)
    q_w1 = A("q_w1", [128, II], F32)
    q_w2 = A("q_w2", [128, II], F32)
    q_sg = A("q_sg", [128, II], F32)
    q_if = A("q_if", [128, II], F32)
    q_sf = A("q_sf", [128, II], F32)
    q_s01 = A("q_s01", [128, II], F32)
    q_out = A("q_out", [128, II], BF16)
    QEND = b._off

    p0_st = [A(f"p0st{i}", [128, II], F32) for i in range(2)]
    p0_stbf = [b.alias_sbuf(f"p0stbf{i}", [128, 1024], BF16, p0_st[i])
               for i in range(2)]
    p0_ev = [A(f"p0ev{i}", [128, 1024], BF16) for i in range(2)]

    b.overlay_reset()
    bigT = A("bigT", [128, FC * S], BF16)
    atok = A("atok", [128, TT * H], BF16)
    vaug = A("vaug", [128, TT * NH * 65], BF16)
    qtp = A("qtp", [128, S], BF16)
    ktp = A("ktp", [128, S], BF16)
    wslq = [A(f"wslq{i}", [128, FC * 128], BF16) for i in range(2)]
    wslk = [A(f"wslk{i}", [128, FC * 128], BF16) for i in range(2)]
    wfull = A("wfull", [128, FC * H], BF16)
    ptr_ = [A(f"ptr{i}", [128, QCW], BF16) for i in range(2)]
    ptm = [A(f"ptm{i}", [128, QCW], BF16) for i in range(4)]
    mbuf = [A(f"mbuf{i}", [128, QCW], BF16) for i in range(2)]
    xlnb = [A(f"xlnb{i}", [128, H], BF16) for i in range(2)]
    dum_act = A("dum_act", [128, H], F32)

    b.overlay_reset()
    w2g = A("w2g", [128, FC * II], BF16)
    w2u = A("w2u", [128, FC * II], BF16)
    w2d = A("w2d", [128, ICC * H], BF16)
    p2_aa = A("p2_aa", [128, H], F32)
    p2_vv = A("p2_vv", [128, H], F32)
    p2_w0 = A("p2_w0", [128, H], F32)
    p2_w1 = A("p2_w1", [128, H], F32)
    p2_w2 = A("p2_w2", [128, H], F32)
    p2_sg = A("p2_sg", [128, H], F32)
    p2_if = A("p2_if", [128, H], F32)
    p2_sf = A("p2_sf", [128, H], F32)
    p2_s01 = A("p2_s01", [128, H], F32)
    xq_sb = A("xq_sb", [128, H], BF16)
    xqT = A("xqT", [128, H], BF16)
    xln2 = A("xln2", [128, H], F32)
    hh = A("hh", [128, 512], F32)
    sg_sb = A("sg_sb", [128, 512], F32)
    hq = A("hq", [128, 512], BF16)
    hqT = A("hqT", [128, 512], BF16)

    arena_size = b.finalize_arena()

    psA = nc.psum_tensor("psA", [128, 1024], F32).__enter__()
    psB = nc.psum_tensor("psB", [128, 1024], F32).__enter__()
    psC = nc.psum_tensor("psC", [128, 512], F32).__enter__()
    psC2 = nc.psum_tensor("psC2", [128, 512], F32).__enter__()
    psD = nc.psum_tensor("psD", [128, 512], F32).__enter__()
    psE = nc.psum_tensor("psE", [128, 512], F32).__enter__()
    psAbf_h = nc.virtual_psum_tensor("psAbf", [128, 2048], BF16)
    psBbf_h = nc.virtual_psum_tensor("psBbf", [128, 2048], BF16)
    for vh, rh in ((psAbf_h, psA), (psBbf_h, psB)):
        nc._tensor(vh.name, vh.shape, vh.dtype, type="PSUM")
        mloc = nc.lookup_mloc(vh)
        mloc.allocated = True
        mloc.addr = nc.lookup_mloc(rh).addr
    psA_bf = psAbf_h.ap()
    psB_bf = psBbf_h.ap()

    s_qv = sem("qv")
    s_qa = sem("qa")

    def gen_quant(src_ap, n, nb, out_ap, bufs, pre_wait=(), out_wait=()):
        aa, vv, w0, w1, w2, sg, invf, s2f, s01 = (
            bufs["aa"][:, 0:n], bufs["vv"][:, 0:n], bufs["w0"][:, 0:n],
            bufs["w1"][:, 0:n], bufs["w2"][:, 0:n], bufs["sg"][:, 0:n],
            bufs["invf"][:, 0:n], bufs["s2f"][:, 0:n], bufs["s01"][:, 0:n])
        bmv = qs_bm.ap()[:, 0:nb]
        bmpv = qs_bmp.ap()[:, 0:nb]
        invv = qs_inv.ap()[:, 0:nb]
        i6v = qs_i6.ap()[:, 0:nb]
        s2v = qs_s2.ap()[:, 0:nb]
        E("scalar", lambda e: e.activation(aa, src_ap, AF.Abs),
          wait=list(pre_wait) + [(s_qv, s_qv.count)], inc=(s_qa, 1))
        E("scalar", lambda e: e.activation(sg, src_ap, AF.Sign),
          wait=[], inc=(s_qa, 1))
        qa_c = s_qa.count

        def st(fn, extra=()):
            E("vector", fn, wait=[(s_qv, s_qv.count)] + list(extra), inc=(s_qv, 1))

        st(lambda e: e.tensor_reduce(bmv, aa.rearrange("p (bb k) -> p bb k", k=16),
                                     AX.X, ALU.max), extra=[(s_qa, qa_c - 1)])
        st(lambda e: ts_(e, bmpv, bmv, 6e-12, None, ALU.max))
        st(lambda e: e.reciprocal(invv, bmpv))
        st(lambda e: ts_(e, i6v, invv, 6.0, None, ALU.mult))
        st(lambda e: ts_(e, s2v, bmpv, 1.0 / 6.0, None, ALU.mult))
        st(lambda e, nb_=nb: e.tensor_copy(
            invf.rearrange("p (bb e2) -> p e2 bb", bb=nb_),
            i6v.rearrange("p (o bb) -> p o bb", o=1).broadcast_to([128, 16, nb_])))
        st(lambda e, nb_=nb: e.tensor_copy(
            s2f.rearrange("p (bb e2) -> p e2 bb", bb=nb_),
            s2v.rearrange("p (o bb) -> p o bb", o=1).broadcast_to([128, 16, nb_])))
        st(lambda e: e.tensor_tensor(vv, aa, invf, ALU.mult))
        vv_read = s_qv.count
        st(lambda e: ts_(e, w0, vv, RC0, RC0, ALU.add, ALU.subtract))
        st(lambda e: ts_(e, w1, vv, RC1, RC1, ALU.add, ALU.subtract))
        st(lambda e: ts_(e, w2, vv, RC2, RC2, ALU.add, ALU.subtract))
        st(lambda e: ts_(e, vv, w0, 2.0, None, ALU.min))
        st(lambda e: ts_(e, w0, w1, 2.0, 4.0, ALU.max, ALU.min))
        st(lambda e: ts_(e, w1, w2, 4.0, 6.0, ALU.max, ALU.min))
        st(lambda e: e.tensor_tensor(s01, vv, w0, ALU.add))
        st(lambda e: e.scalar_tensor_tensor(w2, w1, 6.0, s01, ALU.subtract, ALU.add))
        st(lambda e: e.tensor_tensor(w0, w2, s2f, ALU.mult))
        E("vector", lambda e: e.scalar_tensor_tensor(out_ap, sg, 1.0, w0,
                                                     ALU.mult, ALU.mult),
          wait=[(s_qv, s_qv.count), (s_qa, qa_c)] + list(out_wait), inc=(s_qv, 1))
        return vv_read, s_qv.count, qa_c

    s_wld = sem("wld")
    s_p0mm = sem("p0mm")
    s_p0ev = sem("p0ev")
    s_p0st = sem("p0st")
    s_ini = sem("ini")

    E("vector", lambda e: e.memset(consts.ap()[:, 0:1], 1e-5), inc=(s_ini, 1))
    E("vector", lambda e: e.memset(consts.ap()[:, 1:2], -EXPB),
      wait=[(s_ini, s_ini.count)], inc=(s_ini, 1))
    E("vector", lambda e: e.memset(zbf.ap(), 0.0),
      wait=[(s_ini, s_ini.count)], inc=(s_ini, 1))
    INI = s_ini.count
    b.dma("sync", ident_sb.ap(), ident_in, s_wld)

    p0bufs = {"aa": q_aa.ap(), "vv": q_vv.ap(), "w0": q_w0.ap(), "w1": q_w1.ap(),
              "w2": q_w2.ap(), "sg": q_sg.ap(), "invf": q_if.ap(),
              "s2f": q_sf.ap(), "s01": q_s01.ap()}

    def p0_transpose_store(src_bf_ap, ncol_chunks, dst, dst_rowstride, ot,
                           ready_wait):
        ngrp = (ncol_chunks + 7) // 8
        for g in range(ngrp):
            k0, k1 = g * 8, min(g * 8 + 8, ncol_chunks)
            nblk = k1 - k0
            for k in range(k0, k1):
                E("tensor", lambda e, k=k, k0=k0, src=src_bf_ap: e.transpose(
                    psA_bf[:, (k - k0) * 128:(k - k0 + 1) * 128],
                    src[:, k * 128:(k + 1) * 128], ident_sb.ap()),
                  wait=(list(ready_wait) + b.ps_wait("psA") +
                        [(s_wld, s_wld.count)]) if k == k0 else [],
                  inc=((s_p0mm, 1) if k == k1 - 1 else None))
            mm = s_p0mm.count
            ev = p0_ev[g % 2]
            E("vector", lambda e, nblk=nblk, ev=ev: e.tensor_copy(
                ev.ap()[:, 0:nblk * 128], psA_bf[:, 0:nblk * 128]),
              wait=[(s_p0mm, mm), (s_p0st, max(0, s_p0st.count - 16))],
              inc=(s_p0ev, 1))
            b.ps_read("psA", s_p0ev, s_p0ev.count)
            evc = s_p0ev.count
            dst_ap = bass.AP(dst.tensor, (k0 * 128) * dst_rowstride + ot * 128,
                             [[dst_rowstride, 128], [128 * dst_rowstride, nblk],
                              [1, 128]])
            E("sync", lambda e, o=dst_ap, i=ev.ap()[:, 0:nblk * 128].rearrange(
                "p (k c) -> p k c", k=nblk): e.dma_start(o, i),
              wait=[(s_p0ev, evc)], inc=(s_p0st, 16))

    for nm in ("wq", "wk", "wv", "wo"):
        src, dst = w_in[nm], wT[nm + "T"]
        for ot in range(FC):
            bufi = ot % 2
            st_bf = p0_stbf[bufi].ap()
            wpre = [(s_p0mm, s_p0mm.count)]
            E("gpsimd", lambda e, o=st_bf, i=src[ot * 128:(ot + 1) * 128, :]:
              e.dma_start(o, i), wait=wpre, inc=(s_wld, 16))
            ld = s_wld.count
            p0_transpose_store(st_bf, FC, dst, H, ot, [(s_wld, ld)])

    for nm, rows, cols in (("wg", II, H), ("wu", II, H), ("wd", H, II)):
        src, dst = w_in[nm], wT[nm + "T"]
        nb = cols // 16
        for ot in range(rows // 128):
            bufi = ot % 2
            stg = p0_st[bufi].ap()[:, 0:cols]
            wpre = [(s_qv, s_qv.count), (s_qa, s_qa.count)]
            E("sync", lambda e, o=stg, i=src[ot * 128:(ot + 1) * 128, :]:
              e.dma_start(o, i), wait=wpre, inc=(s_wld, 16))
            ld = s_wld.count
            qdst = q_out.ap()[:, 0:cols]
            _, qn, _ = gen_quant(stg, cols, nb, qdst, p0bufs,
                                 pre_wait=[(s_wld, ld)],
                                 out_wait=[(s_p0mm, s_p0mm.count)])
            p0_transpose_store(qdst, cols // 128, dst,
                               rows, ot, [(s_qv, qn)])

    P0_ST = s_p0st.count

    s_x = sem("x")
    s_ln = sem("ln")
    s_lnv = sem("lnv")
    s_ap1 = sem("ap1")
    s_xt = sem("xt")
    s_xte = sem("xte")
    s_wf = sem("wf")
    s_vps = sem("vps")
    s_vev = sem("vev")
    s_qkl = sem("qkl")
    s_qkm = sem("qkm")
    s_qke = sem("qke")
    s_sa = sem("sa")
    s_sb = sem("sb")
    s_ea = sem("ea")
    s_eb = sem("eb")
    s_ma = sem("ma")
    s_mb = sem("mb")
    s_pva = sem("pva")
    s_pvb = sem("pvb")
    s_pvn = sem("pvn")
    s_mld = sem("mld")
    s_wo = sem("wo")
    s_res = sem("res")
    s_x2w = sem("x2w")

    def ln_stats(xap, m_slot, r_slot, pre_wait):
        su = lnsm.ap()[:, 0:1]
        sq = lnsm.ap()[:, 1:2]
        var = lnsm.ap()[:, 3:4]
        lnv = lnsm.ap()[:, 4:5]
        E("scalar", lambda e: e.activation(dum_act.ap(), xap, AF.Copy, accum_out=su),
          wait=list(pre_wait) + [(s_lnv, s_lnv.count)], inc=(s_ln, 1))
        E("scalar", lambda e: e.activation(dum_act.ap(), xap, AF.Square, accum_out=sq),
          wait=[], inc=(s_ln, 1))
        lc = s_ln.count
        E("vector", lambda e, m=m_slot: ts_(e, m, su, 1.0 / H, None, ALU.mult),
          wait=[(s_ln, lc)], inc=(s_lnv, 1))
        E("vector", lambda e, m=m_slot: e.tensor_tensor(var, m, m, ALU.mult),
          wait=[(s_lnv, s_lnv.count)], inc=(s_lnv, 1))
        E("vector", lambda e: e.scalar_tensor_tensor(var, sq, 1.0 / H, var,
                                                     ALU.mult, ALU.subtract),
          wait=[(s_lnv, s_lnv.count)], inc=(s_lnv, 1))
        vc = s_lnv.count
        E("scalar", lambda e: e.activation(lnv, var, AF.Ln, bias=consts.ap()[:, 0:1]),
          wait=[(s_lnv, vc), (s_ini, INI)], inc=(s_ln, 1))
        E("scalar", lambda e, r=r_slot: e.activation(r, lnv, AF.Exp, scale=-0.5),
          wait=[(s_ln, s_ln.count)], inc=(s_ln, 1))
        return s_ln.count

    NPAIR = NH // 2
    mask_hist = []

    for sqi in range(SEQ_PER_CORE):
        for tt in range(TT):
            bufi = tt % 2
            wpre = [(s_ap1, max(0, s_ap1.count - 1)), (s_res, s_res.count)]
            if sqi == 0 and tt == 0:
                wpre.append((s_p0st, P0_ST))
            E("sync", lambda e, o=xld[bufi].ap(),
              i=x_in[sqi, tt * 128:(tt + 1) * 128, :]: e.dma_start(o, i),
              wait=wpre, inc=(s_x, 16))
            xc = s_x.count
            xap = xld[bufi].ap()
            lc = ln_stats(xap, lnsm.ap()[:, 2:3], lnsm.ap()[:, 7:8],
                          [(s_x, xc)])
            xb_ = xlnb[bufi].ap()
            E("vector", lambda e, xa=xap, o=xb_: ts_(
                e, o, xa, lnsm.ap()[:, 2:3], lnsm.ap()[:, 7:8],
                ALU.subtract, ALU.mult),
              wait=[(s_ln, lc), (s_xt, max(0, s_xt.count - 1))], inc=(s_ap1, 1))
            ap_c = s_ap1.count
            for k in range(FC):
                E("tensor", lambda e, k=k, i=xb_[:, k * 128:(k + 1) * 128]:
                  e.transpose(psA_bf[:, k * 128:(k + 1) * 128], i, ident_sb.ap()),
                  wait=([(s_ap1, ap_c)] + b.ps_wait("psA")) if k == 0 else [],
                  inc=((s_xt, 1) if k == FC - 1 else None))
            mmc = s_xt.count
            E("vector", lambda e, tt=tt: e.tensor_copy(
                bigT.ap().rearrange("p (c t) -> p c t", c=FC)
                [:, :, tt * 128:(tt + 1) * 128],
                psA_bf[:, 0:1024].rearrange("p (c t) -> p c t", c=FC)),
              wait=[(s_xt, mmc)], inc=(s_xte, 1))
            b.ps_read("psA", s_xte, s_xte.count)
        XTE = s_xte.count
        if debug and sqi == 0:
            b.dma("sync", dbg["d_bigT"], bigT.ap(), sem("dbg"),
                  wait=[(s_xte, XTE)])
            b.dma("sync", dbg["d_wqT"], wT["wqT"], sem("dbg"),
                  wait=[(s_p0st, P0_ST)])

        b.dma("sync", wfull.ap().rearrange("p (c n) -> p c n", c=FC),
              wT["wvT"].rearrange("(c p) n -> p c n", p=128), s_wf,
              wait=[(s_p0st, P0_ST), (s_sa, s_sa.count), (s_sb, s_sb.count),
                    (s_wo, s_wo.count)])
        wfc = s_wf.count
        E("vector", lambda e: e.memset(
            vaug.ap().rearrange("p (t h j) -> p t h j", h=NH, j=65)[:, :, :, 64:65],
            1.0),
          wait=[(s_pvn, s_pvn.count)], inc=(s_vev, 1))
        for tt in range(TT):
            for nch in range(2):
                for fc in range(FC):
                    E("tensor", lambda e, tt=tt, fc=fc, nch=nch: e.matmul(
                        psB[:, nch * 512:(nch + 1) * 512],
                        bigT.ap().rearrange("p (c t) -> p c t", c=FC)
                        [:, fc, tt * 128:(tt + 1) * 128],
                        wfull.ap().rearrange("p (c n) -> p c n", c=FC)
                        [:, fc, nch * 512:(nch + 1) * 512],
                        start=(fc == 0), stop=(fc == FC - 1)),
                      wait=([(s_wf, wfc), (s_xte, XTE)] + b.ps_wait("psB"))
                      if (fc == 0 and nch == 0) else [],
                      inc=((s_vps, 1) if fc == FC - 1 and nch == 1 else None))
            vp = s_vps.count
            E("vector", lambda e, tt=tt: e.tensor_copy(
                vaug.ap().rearrange("p (t h j) -> p t h j", h=NH, j=65)
                [:, tt, :, 0:64],
                psB[:, :].rearrange("p (h j) -> p h j", h=NH)),
              wait=[(s_vps, vp)], inc=(s_vev, 1))
            b.ps_read("psB", s_vev, s_vev.count)
        VEV = s_vev.count
        if debug and sqi == 0:
            b.dma("sync", dbg["d_vaug"], vaug.ap(), sem("dbg"),
                  wait=[(s_vev, VEV)])

        for pair in range(NPAIR):
            mask_t = maskS if pair < NPAIR // 2 else maskF
            pc = pair * 128
            wbq, wbk = wslq[pair % 2], wslk[pair % 2]
            wq_ap = bass.AP(wT["wqT"].tensor, pc, [[H, 128], [128 * H, FC], [1, 128]])
            wk_ap = bass.AP(wT["wkT"].tensor, pc, [[H, 128], [128 * H, FC], [1, 128]])
            wpre = ([(s_qkm, max(0, s_qkm.count - 8))]
                    if (sqi * NPAIR + pair) >= 2 else [(s_p0st, P0_ST)])
            E("sync", lambda e, o=wbq.ap().rearrange("p (c n) -> p c n", c=FC),
              i=wq_ap: e.dma_start(o, i), wait=wpre, inc=(s_qkl, 16))
            E("sync", lambda e, o=wbk.ap().rearrange("p (c n) -> p c n", c=FC),
              i=wk_ap: e.dma_start(o, i), wait=[], inc=(s_qkl, 16))
            qkc = s_qkl.count
            for (wb, outb, psx, psname) in ((wbq, qtp, psC, "psC"),
                                            (wbk, ktp, psC2, "psC2")):
                for nchk in range(4):
                    for fc in range(FC):
                        E("tensor", lambda e, wb=wb, nchk=nchk, fc=fc, psx=psx:
                          e.matmul(
                            psx[:, :],
                            wb.ap().rearrange("p (c n) -> p c n", c=FC)[:, fc, :],
                            bigT.ap().rearrange("p (c t) -> p c t", c=FC)
                            [:, fc, nchk * 512:(nchk + 1) * 512],
                            start=(fc == 0), stop=(fc == FC - 1)),
                          wait=([(s_qkl, qkc), (s_xte, XTE)] + b.ps_wait(psname))
                          if fc == 0 else [],
                          inc=((s_qkm, 1) if fc == FC - 1 else None))
                    qm = s_qkm.count
                    E("vector", lambda e, outb=outb, nchk=nchk, psx=psx:
                      e.tensor_copy(outb.ap()[:, nchk * 512:(nchk + 1) * 512],
                                    psx[:, :]),
                      wait=[(s_qkm, qm)], inc=(s_qke, 1))
                    b.ps_read(psname, s_qke, s_qke.count)
            QKE = s_qke.count
            if debug and sqi == 0 and pair == 0:
                b.dma("sync", dbg["d_qtp"], qtp.ap(), sem("dbg"),
                      wait=[(s_qke, QKE)])
                b.dma("sync", dbg["d_ktp"], ktp.ap(), sem("dbg"),
                      wait=[(s_qke, QKE)])

            for qc in range(NQC):
                qlo, qhi = qc * QCW, qc * QCW + QCW
                ktmax = qhi // 128
                ma0, mb0 = s_ma.count, s_mb.count
                pva0, pvb0 = s_pva.count, s_pvb.count
                ml0 = s_mld.count
                for kt in range(ktmax):
                    qstart = max(qlo, kt * 128)
                    n = qhi - qstart
                    c0 = qstart - qlo
                    g = len(mask_hist)
                    mb_ = mbuf[g % 2]
                    wm = []
                    if g >= 2:
                        wm = [(s_ma, mask_hist[g - 2][0]),
                              (s_mb, mask_hist[g - 2][1])]
                    E("sync", lambda e, o=mb_.ap()[:, 0:n],
                      i=mask_t[kt * 128:(kt + 1) * 128, qstart:qhi]:
                      e.dma_start(o, i), wait=wm, inc=(s_mld, 16))
                    mldc = s_mld.count
                    mask_hist.append(None)
                    for hb in (0, 1):
                        hg = pair * 2 + hb
                        psS = psA if hb == 0 else psB
                        psSn = "psA" if hb == 0 else "psB"
                        s_s = s_sa if hb == 0 else s_sb
                        s_e = s_ea if hb == 0 else s_eb
                        s_m = s_ma if hb == 0 else s_mb
                        s_pv = s_pva if hb == 0 else s_pvb
                        pv0 = pva0 if hb == 0 else pvb0
                        pt_raw = ptr_[hb]
                        pt_m = ptm[2 * hb + (g % 2)]
                        segs = []
                        cpos = c0
                        while cpos < c0 + n:
                            seg_end = min(c0 + n, (cpos // 512 + 1) * 512)
                            segs.append((cpos, seg_end - cpos))
                            cpos = seg_end
                        for si, (cs, cw) in enumerate(segs):
                            E("tensor", lambda e, kt=kt, hb=hb, cs=cs, cw=cw,
                              qlo=qlo, psS=psS: e.matmul(
                                psS[:, cs:cs + cw],
                                ktp.ap()[64 * hb:64 * hb + 64,
                                         kt * 128:(kt + 1) * 128],
                                qtp.ap()[64 * hb:64 * hb + 64,
                                         qlo + cs:qlo + cs + cw],
                                start=True, stop=True),
                              wait=([(s_qke, QKE)] + b.ps_wait(psSn))
                              if si == 0 else [],
                              inc=((s_s, 1) if si == len(segs) - 1 else None))
                        sc = s_s.count
                        E("scalar", lambda e, psS=psS, c0=c0, n=n, pt=pt_raw:
                          e.activation(pt.ap()[:, c0:c0 + n], psS[:, c0:c0 + n],
                                       AF.Exp, bias=consts.ap()[:, 1:2],
                                       scale=SCALE),
                          wait=[(s_s, sc), (s_m, s_m.count), (s_ini, INI)],
                          inc=(s_e, 1))
                        b.ps_read(psSn, s_e, s_e.count)
                        ec = s_e.count
                        wmm = [(s_e, ec), (s_mld, mldc)]
                        if kt >= 2:
                            wmm.append((s_pv, pv0 + kt - 1))
                        E("vector", lambda e, pt=pt_raw, pm=pt_m, mb2=mb_,
                          c0=c0, n=n: e.tensor_tensor(
                            pm.ap()[:, c0:c0 + n], pt.ap()[:, c0:c0 + n],
                            mb2.ap()[:, 0:n], ALU.mult),
                          wait=wmm, inc=(s_m, 1))
                        mc = s_m.count
                        if (debug and sqi == 0 and pair == 0 and qc == 0
                                and kt == 0 and hb == 0):
                            b.dma("sync", dbg["d_ptm"], pt_m.ap(), sem("dbg"),
                                  wait=[(s_m, mc)])
                        if (debug and sqi == 0 and pair == 0 and qc == 0
                                and kt == 7 and hb == 0):
                            b.dma("sync", dbg["d_ptm7"], pt_m.ap(), sem("dbg"),
                                  wait=[(s_m, mc)])
                        first_q = max(kt, qlo // 128)
                        for qg in range(first_q, ktmax):
                            ql = qg - qlo // 128
                            psP, psPn = ((psC, "psC"), (psC2, "psC2"),
                                         (psD, "psD"), (psE, "psE"))[hb * 2 + ql // 4]
                            slot = (ql % 4) * 65
                            wpv = []
                            if qg == first_q:
                                wpv = [(s_m, mc), (s_vev, VEV)]
                                if kt == 0:
                                    wpv += b.ps_wait(psPn)
                                    if ql // 4 == 0:
                                        wpv += b.ps_wait(
                                            ("psC2", "psE")[hb])
                            E("tensor", lambda e, pm=pt_m, qg=qg, qlo=qlo,
                              hg=hg, kt=kt, psP=psP, slot=slot, ql=ql:
                              e.matmul(
                                psP[:, slot:slot + 65],
                                pm.ap()[:, qg * 128 - qlo:qg * 128 - qlo + 128],
                                vaug.ap()[:, kt * NH * 65 + hg * 65:
                                          kt * NH * 65 + hg * 65 + 65],
                                start=(kt == 0 and ql in (0, 4)),
                                stop=(kt == qg)),
                              wait=wpv,
                              inc=((s_pv, 1) if qg == ktmax - 1 else None))
                        mask_hist[g] = (s_ma.count, s_mb.count)
                for hb, s_pv in ((0, s_pva), (1, s_pvb)):
                    ps1, ps2 = (psC, psC2) if hb == 0 else (psD, psE)
                    E("tensor", lambda e, ps1=ps1: e.matmul(
                        ps1[:, 0:260], zbf.ap()[:, 0:128], zbf.ap()[:, 0:260],
                        start=False, stop=True),
                      wait=[(s_ini, INI)])
                    E("tensor", lambda e, ps2=ps2: e.matmul(
                        ps2[:, 0:260], zbf.ap()[:, 0:128], zbf.ap()[:, 0:260],
                        start=False, stop=True),
                      wait=[], inc=(s_pv, 1))
                if debug and sqi == 0 and pair == 0 and qc == 0:
                    _pvdbg = xlnb[0].ap()[:, 0:520].bitcast(F32)[:, 0:520] \
                        if False else None
                    E("vector", lambda e: e.tensor_copy(
                        dum_act.ap()[:, 0:260], psC[:, 0:260]),
                      wait=[(s_pva, s_pva.count)], inc=(sem("dbgv"), 1))
                    E("vector", lambda e: e.tensor_copy(
                        dum_act.ap()[:, 260:520], psC2[:, 0:260]),
                      wait=[(sem("dbgv"), sem("dbgv").count)], inc=(sem("dbgv"), 1))
                    b.dma("sync", dbg["d_pv"], dum_act.ap()[:, 0:520],
                          sem("dbg"), wait=[(sem("dbgv"), sem("dbgv").count)])
                for hb in (0, 1):
                    hg = pair * 2 + hb
                    s_pv = s_pva if hb == 0 else s_pvb
                    pvc = s_pv.count
                    for ql in range(QCW // 128):
                        qg = qlo // 128 + ql
                        psP, psPn = ((psC, "psC"), (psC2, "psC2"),
                                     (psD, "psD"), (psE, "psE"))[hb * 2 + ql // 4]
                        slot = (ql % 4) * 65
                        rr = pvr.ap()[:, hb * 8 + ql:hb * 8 + ql + 1]
                        w1_ = [(s_pvn, s_pvn.count)]
                        if ql == 0:
                            w1_.append((s_pv, pvc))
                        E("vector", lambda e, psP=psP, slot=slot, rr=rr:
                          e.reciprocal(rr, psP[:, slot + 64:slot + 65]),
                          wait=w1_, inc=(s_pvn, 1))
                        E("vector", lambda e, psP=psP, slot=slot, rr=rr,
                          qg=qg, hg=hg: ts_(
                            e, atok.ap().rearrange("p (t c) -> p t c", t=TT)
                            [:, qg, hg * 64:(hg + 1) * 64],
                            psP[:, slot:slot + 64], rr, None, ALU.mult),
                          wait=[(s_pvn, s_pvn.count)], inc=(s_pvn, 1))
                        b.ps_read(psPn, s_pvn, s_pvn.count)
        ATOK = s_pvn.count
        if debug and sqi == 0:
            b.dma("sync", dbg["d_atok"], atok.ap(), sem("dbg"),
                  wait=[(s_pvn, ATOK)])

        for tt in range(TT):
            for k in range(FC):
                E("tensor", lambda e, tt=tt, k=k: e.transpose(
                    psA_bf[:, k * 128:(k + 1) * 128],
                    atok.ap()[:, tt * 1024 + k * 128:tt * 1024 + (k + 1) * 128],
                    ident_sb.ap()),
                  wait=([(s_pvn, ATOK)] + b.ps_wait("psA")) if k == 0 else [],
                  inc=((s_xt, 1) if k == FC - 1 else None))
            atc = s_xt.count
            E("vector", lambda e, tt=tt: e.tensor_copy(
                bigT.ap().rearrange("p (c t) -> p c t", c=FC)
                [:, :, tt * 128:(tt + 1) * 128],
                psA_bf[:, 0:1024].rearrange("p (c t) -> p c t", c=FC)),
              wait=[(s_xt, atc)], inc=(s_xte, 1))
            b.ps_read("psA", s_xte, s_xte.count)
        ATE = s_xte.count

        b.dma("sync", wfull.ap().rearrange("p (c n) -> p c n", c=FC),
              wT["woT"].rearrange("(c p) n -> p c n", p=128), s_wf,
              wait=[(s_vps, s_vps.count), (s_sa, s_sa.count), (s_sb, s_sb.count)])
        wfc = s_wf.count
        for tt in range(TT):
            bufi = tt % 2
            E("sync", lambda e, o=xld[bufi].ap(),
              i=x_in[sqi, tt * 128:(tt + 1) * 128, :]: e.dma_start(o, i),
              wait=[(s_ap1, s_ap1.count), (s_res, max(0, s_res.count - 2))],
              inc=(s_x, 16))
            xc = s_x.count
            for ncb in range(2):
                psW, psWn = ((psC, "psC"), (psC2, "psC2"))[ncb]
                for fc in range(FC):
                    E("tensor", lambda e, tt=tt, fc=fc, ncb=ncb, psW=psW:
                      e.matmul(
                        psW[:, :],
                        bigT.ap().rearrange("p (c t) -> p c t", c=FC)
                        [:, fc, tt * 128:(tt + 1) * 128],
                        wfull.ap().rearrange("p (c n) -> p c n", c=FC)
                        [:, fc, ncb * 512:(ncb + 1) * 512],
                        start=(fc == 0), stop=(fc == FC - 1)),
                      wait=([(s_wf, wfc), (s_xte, ATE)] + b.ps_wait(psWn))
                      if fc == 0 else [],
                      inc=((s_wo, 1) if fc == FC - 1 else None))
            woc = s_wo.count
            x2b = x2w[bufi]
            E("vector", lambda e, x2b=x2b, bufi=bufi: e.tensor_tensor(
                x2b.ap()[:, 0:512], psC[:, :], xld[bufi].ap()[:, 0:512], ALU.add),
              wait=[(s_wo, woc), (s_x, xc), (s_x2w, max(0, s_x2w.count - 16))],
              inc=(s_res, 1))
            E("vector", lambda e, x2b=x2b, bufi=bufi: e.tensor_tensor(
                x2b.ap()[:, 512:1024], psC2[:, :], xld[bufi].ap()[:, 512:1024],
                ALU.add),
              wait=[(s_res, s_res.count)], inc=(s_res, 1))
            rc = s_res.count
            b.ps_read("psC", s_res, rc)
            b.ps_read("psC2", s_res, rc)
            gt = sqi * TT + tt
            lc = ln_stats(x2b.ap(), ln2m.ap()[:, gt:gt + 1],
                          ln2r.ap()[:, gt:gt + 1], [(s_res, rc)])
            E("sync", lambda e, x2b=x2b, sqi=sqi, tt=tt: e.dma_start(
                x2s[sqi, tt * 128:(tt + 1) * 128, :], x2b.ap()),
              wait=[(s_ln, lc)], inc=(s_x2w, 16))

    P1_X2W = s_x2w.count
    P1_LN = s_ln.count

    s_w2 = sem("w2")
    s_x2r = sem("x2r")
    s_t2 = sem("t2")
    s_te2 = sem("te2")
    s_gu = sem("gu")
    s_sil = sem("sil")
    s_hm = sem("hm")
    s_dn = sem("dn")
    s_ov = sem("ov")
    s_ow = sem("ow")
    s_ap2 = sem("ap2")

    p2bufs = {"aa": p2_aa.ap(), "vv": p2_vv.ap(), "w0": p2_w0.ap(),
              "w1": p2_w1.ap(), "w2": p2_w2.ap(), "sg": p2_sg.ap(),
              "invf": p2_if.ap(), "s2f": p2_sf.ap(), "s01": p2_s01.ap()}

    b.dma("sync", w2g.ap().rearrange("p (c n) -> p c n", c=FC),
          wT["wgT"].rearrange("(c p) n -> p c n", p=128), s_w2,
          wait=[(s_x2w, P1_X2W), (s_qkm, s_qkm.count), (s_wo, s_wo.count),
                (s_pvn, s_pvn.count), (s_xte, s_xte.count),
                (s_ma, s_ma.count), (s_mb, s_mb.count)])
    b.dma("sync", w2u.ap().rearrange("p (c n) -> p c n", c=FC),
          wT["wuT"].rearrange("(c p) n -> p c n", p=128), s_w2)
    b.dma("sync", w2d.ap().rearrange("p (c n) -> p c n", c=ICC),
          wT["wdT"].rearrange("(c p) n -> p c n", p=128), s_w2)
    W2C = s_w2.count

    IC_SIZES = [512, 512, 512, 512, 512, 256]
    IC_OFF = [0, 512, 1024, 1536, 2048, 2560]

    for gt in range(SEQ_PER_CORE * TT):
        sqi, tt = divmod(gt, TT)
        bufi = gt % 2
        E("sync", lambda e, o=xld[bufi].ap(),
          i=x2s[sqi, tt * 128:(tt + 1) * 128, :]: e.dma_start(o, i),
          wait=[(s_x2w, P1_X2W), (s_ap2, max(0, s_ap2.count - 1)),
                (s_ov, max(0, s_ov.count - 2))],
          inc=(s_x2r, 16))
        xc = s_x2r.count
        E("vector", lambda e, bufi=bufi, gt=gt: ts_(
            e, xln2.ap(), xld[bufi].ap(),
            ln2m.ap()[:, gt:gt + 1], ln2r.ap()[:, gt:gt + 1],
            ALU.subtract, ALU.mult),
          wait=[(s_x2r, xc), (s_ln, P1_LN), (s_qv, s_qv.count),
                (s_qa, s_qa.count), (s_w2, W2C)],
          inc=(s_ap2, 1))
        apc = s_ap2.count
        _, qn, _ = gen_quant(xln2.ap(), H, H // 16, xq_sb.ap(), p2bufs,
                             pre_wait=[(s_ap2, apc)],
                             out_wait=[(s_t2, s_t2.count)])
        for k in range(FC):
            E("tensor", lambda e, k=k: e.transpose(
                psA_bf[:, k * 128:(k + 1) * 128],
                xq_sb.ap()[:, k * 128:(k + 1) * 128], ident_sb.ap()),
              wait=([(s_qv, qn)] + b.ps_wait("psA")) if k == 0 else [],
              inc=((s_t2, 1) if k == FC - 1 else None))
        tc = s_t2.count
        E("vector", lambda e: e.tensor_copy(xqT.ap(), psA_bf[:, 0:1024]),
          wait=[(s_t2, tc)], inc=(s_te2, 1))
        b.ps_read("psA", s_te2, s_te2.count)
        XQT = s_te2.count

        for ic in range(6):
            icw, ico = IC_SIZES[ic], IC_OFF[ic]
            for gi, (ps_, psn, wsb) in enumerate(((psC, "psC", w2g),
                                                  (psC2, "psC2", w2u))):
                for fc in range(FC):
                    E("tensor", lambda e, ps_=ps_, wsb=wsb, fc=fc, ico=ico,
                      icw=icw: e.matmul(
                        ps_[:, 0:icw],
                        xqT.ap().rearrange("p (c t) -> p c t", c=FC)[:, fc, :],
                        wsb.ap().rearrange("p (c n) -> p c n", c=FC)
                        [:, fc, ico:ico + icw],
                        start=(fc == 0), stop=(fc == FC - 1)),
                      wait=([(s_te2, XQT)] + b.ps_wait(psn)) if fc == 0 else [],
                      inc=((s_gu, 1) if fc == FC - 1 else None))
            guc = s_gu.count
            E("scalar", lambda e, icw=icw: e.activation(
                sg_sb.ap()[:, 0:icw], psC[:, 0:icw], AF.Silu),
              wait=[(s_gu, guc), (s_hm, s_hm.count)], inc=(s_sil, 1))
            b.ps_read("psC", s_sil, s_sil.count)
            slc = s_sil.count
            E("vector", lambda e, icw=icw: e.tensor_tensor(
                hh.ap()[:, 0:icw], sg_sb.ap()[:, 0:icw], psC2[:, 0:icw],
                ALU.mult),
              wait=[(s_sil, slc), (s_qv, s_qv.count), (s_qa, s_qa.count)],
              inc=(s_hm, 1))
            b.ps_read("psC2", s_hm, s_hm.count)
            hmc = s_hm.count
            _, q2n, _ = gen_quant(hh.ap()[:, 0:icw], icw, icw // 16,
                                  hq.ap()[:, 0:icw], p2bufs,
                                  pre_wait=[(s_hm, hmc)],
                                  out_wait=[(s_t2, s_t2.count)])
            nsub = icw // 128
            for k in range(nsub):
                E("tensor", lambda e, k=k: e.transpose(
                    psB_bf[:, k * 128:(k + 1) * 128],
                    hq.ap()[:, k * 128:(k + 1) * 128], ident_sb.ap()),
                  wait=([(s_qv, q2n)] + b.ps_wait("psB")) if k == 0 else [],
                  inc=((s_t2, 1) if k == nsub - 1 else None))
            tc2 = s_t2.count
            E("vector", lambda e, icw=icw: e.tensor_copy(
                hqT.ap()[:, 0:icw], psB_bf[:, 0:icw]),
              wait=[(s_t2, tc2)], inc=(s_te2, 1))
            b.ps_read("psB", s_te2, s_te2.count)
            hqt = s_te2.count
            for k in range(nsub):
                isub = ico // 128 + k
                for ncb in range(2):
                    psX, psXn = ((psD, "psD"), (psE, "psE"))[ncb]
                    wdn = []
                    if k == 0 and ncb == 0:
                        wdn = [(s_te2, hqt)]
                    if isub == 0:
                        wdn = wdn + b.ps_wait(psXn)
                    E("tensor", lambda e, k=k, isub=isub, ncb=ncb, psX=psX:
                      e.matmul(
                        psX[:, :],
                        hqT.ap()[:, k * 128:(k + 1) * 128],
                        w2d.ap().rearrange("p (c n) -> p c n", c=ICC)
                        [:, isub, ncb * 512:(ncb + 1) * 512],
                        start=(isub == 0), stop=(isub == ICC - 1)),
                      wait=wdn,
                      inc=((s_dn, 1) if (isub == ICC - 1 and ncb == 1) else None))
        dnc = s_dn.count
        ob = x2w[bufi]
        E("vector", lambda e, ob=ob, bufi=bufi: e.tensor_tensor(
            ob.ap()[:, 0:512], psD[:, :], xld[bufi].ap()[:, 0:512], ALU.add),
          wait=[(s_dn, dnc), (s_ow, max(0, s_ow.count - 16))], inc=(s_ov, 1))
        E("vector", lambda e, ob=ob, bufi=bufi: e.tensor_tensor(
            ob.ap()[:, 512:1024], psE[:, :], xld[bufi].ap()[:, 512:1024],
            ALU.add),
          wait=[(s_ov, s_ov.count)], inc=(s_ov, 1))
        ovc = s_ov.count
        b.ps_read("psD", s_ov, ovc)
        b.ps_read("psE", s_ov, ovc)
        E("sync", lambda e, ob=ob, sqi=sqi, tt=tt: e.dma_start(
            out_hbm[sqi, tt * 128:(tt + 1) * 128, :], ob.ap()),
          wait=[(s_ov, ovc)], inc=(s_ow, 16))

    OW = s_ow.count
    E("sync", lambda e: e.wait_ge(b.sems["ow"].h, OW))

    for s in b.sems.values():
        s.h = nc.semaphore(s.name).__enter__()

    block = nc.Block().__enter__()

    def runner(lst):
        def _(eng):
            for fn, waits, inc in lst:
                for (s, v) in waits:
                    if v > 0:
                        eng.wait_ge(s.h, v)
                inst = fn(eng)
                if inc is not None:
                    inst.then_inc(inc[0].h, inc[1])
        return _

    block.sync(runner(b.ops["sync"]))
    block.scalar(runner(b.ops["scalar"]))
    block.vector(runner(b.ops["vector"]))
    block.tensor(runner(b.ops["tensor"]))
    block.gpsimd(runner(b.ops["gpsimd"]))
    return nc


_cache = {}


def _build_masks():
    if "m" not in _cache:
        i = np.arange(S)[:, None]
        j = np.arange(S)[None, :]
        causal = j <= i
        strided = causal & ((i - j < STRIDE) | (j % STRIDE == i % STRIDE))
        fixed = causal & ((j // STRIDE == i // STRIDE) | (j % STRIDE >= STRIDE - 1))
        _cache["m"] = (np.ascontiguousarray(strided.T.astype(ml_dtypes.bfloat16)),
                       np.ascontiguousarray(fixed.T.astype(ml_dtypes.bfloat16)))
    return _cache["m"]


def kernel(L0_attn_x, L0_attn_Wq, L0_attn_Wk, L0_attn_Wv, L0_attn_Wo,
           L0_ffn_w_gate, L0_ffn_w_up, L0_ffn_w_down):
    x = np.ascontiguousarray(np.asarray(L0_attn_x, dtype=np.float32))
    mS, mF = _build_masks()
    if "nc" not in _cache:
        _cache["nc"] = build_program()
    nc = _cache["nc"]
    common = {
        "wq": np.ascontiguousarray(np.asarray(L0_attn_Wq, np.float32)),
        "wk": np.ascontiguousarray(np.asarray(L0_attn_Wk, np.float32)),
        "wv": np.ascontiguousarray(np.asarray(L0_attn_Wv, np.float32)),
        "wo": np.ascontiguousarray(np.asarray(L0_attn_Wo, np.float32)),
        "wg": np.ascontiguousarray(np.asarray(L0_ffn_w_gate, np.float32)),
        "wu": np.ascontiguousarray(np.asarray(L0_ffn_w_up, np.float32)),
        "wd": np.ascontiguousarray(np.asarray(L0_ffn_w_down, np.float32)),
        "maskS": mS, "maskF": mF,
        "ident": np.eye(128, dtype=ml_dtypes.bfloat16),
    }
    in_maps = []
    for c in range(NCORES):
        m = dict(common)
        m["x"] = np.ascontiguousarray(x[c * SEQ_PER_CORE:(c + 1) * SEQ_PER_CORE])
        in_maps.append(m)
    res = run_bass_kernel_spmd(nc, in_maps, core_ids=list(range(NCORES)))
    out = np.concatenate([r["out"] for r in res.results], axis=0)
    return out.astype(np.float32)
